# version 16
# baseline (speedup 1.0000x reference)
import contextlib
import numpy as np
import ml_dtypes
import concourse.bass as bass
import concourse.mybir as mybir
from concourse.bass_utils import run_bass_kernel_spmd


F32 = mybir.dt.float32
BF16 = mybir.dt.bfloat16
AF = mybir.ActivationFunctionType
OP = mybir.AluOpType
AX = mybir.AxisListType

GAMMA = 0.99
NCORES = 8
BATCH = 4
SEQ = 4096
SC = SEQ // NCORES
T = SC * BATCH
NT = T // 128
HID = 1024
NHC = HID // 128
MEM = 256
NMC = MEM // 128
NCH = 4
NE = 8
DFF = 4096
NFC = DFF // 128
G128 = float(GAMMA ** 128)


class _S(object):
    __slots__ = ("h", "n")

    def __init__(self, h):
        self.h = h
        self.n = 0


class _Prog(object):
    ENGS = ("sync", "gpsimd", "tensor", "vector", "scalar")

    def __init__(self):
        self.steps = {k: [] for k in self.ENGS}

    def em(self, eng, build, waits=(), inc=None, amt=1):
        w = tuple((s.h, int(v)) for (s, v) in waits if v > 0)
        tgt = None
        if inc is not None:
            inc.n += amt
            tgt = (inc.h, amt)
        self.steps[eng].append((w, build, tgt))
        return None if inc is None else inc.n

    def replay(self, block):
        def mk(steps):
            def run(eng):
                for w, build, tgt in steps:
                    for h, v in w:
                        eng.wait_ge(h, v)
                    ins = build(eng)
                    if tgt is not None:
                        ins.then_inc(tgt[0], tgt[1])
            return run
        block.sync(mk(self.steps["sync"]))
        block.gpsimd(mk(self.steps["gpsimd"]))
        block.tensor(mk(self.steps["tensor"]))
        block.vector(mk(self.steps["vector"]))
        block.scalar(mk(self.steps["scalar"]))


def _np_consts():
    sj = (np.arange(512) // BATCH)[:, None].astype(np.float64)
    si = (np.arange(512) // BATCH)[None, :].astype(np.float64)
    DT = np.where(sj < si, GAMMA ** (si - sj - 1.0) / BATCH, 0.0).astype(np.float32)
    p = np.arange(128)
    gsc = np.zeros((128, 8), dtype=np.float32)
    for rt in range(4):
        s_loc = (32 * rt + p // BATCH).astype(np.float64)
        gsc[:, rt] = (GAMMA ** (127.0 - s_loc) / BATCH).astype(np.float32)
        gsc[:, 4 + rt] = (GAMMA ** s_loc).astype(np.float32)
    return DT, gsc


def build_nc(debug=False):
    nc = bass.Bass("TRN2", target_bir_lowering=False)

    x_d = nc.dram_tensor("x", [T, HID], F32, kind="ExternalInput").ap()
    wqT_d = nc.dram_tensor("wqT", [HID, MEM], F32, kind="ExternalInput").ap()
    wkT_d = nc.dram_tensor("wkT", [HID, MEM], F32, kind="ExternalInput").ap()
    wvT_d = nc.dram_tensor("wvT", [HID, MEM], F32, kind="ExternalInput").ap()
    woT_d = nc.dram_tensor("woT", [MEM, HID], F32, kind="ExternalInput").ap()
    m0T_d = nc.dram_tensor("m0T", [MEM, MEM], F32, kind="ExternalInput").ap()
    gate_d = nc.dram_tensor("gate128", [128, 1], F32, kind="ExternalInput").ap()
    wr_d = nc.dram_tensor("wr", [HID, NE], F32, kind="ExternalInput").ap()
    br_d = nc.dram_tensor("br", [1, NE], F32, kind="ExternalInput").ap()
    b1r_d = nc.dram_tensor("b1r", [128, NE * NFC], F32, kind="ExternalInput").ap()
    b2_d = nc.dram_tensor("b2row", [1, NE * HID], F32, kind="ExternalInput").ap()
    w1_d = nc.dram_tensor("w1", [NE, HID, DFF], F32, kind="ExternalInput").ap()
    w2_d = nc.dram_tensor("w2", [NE, DFF, HID], F32, kind="ExternalInput").ap()
    coefs_d = nc.dram_tensor("coefs", [128, 16], F32, kind="ExternalInput").ap()
    y_d = nc.dram_tensor("y", [T, HID], F32, kind="ExternalOutput").ap()
    ag_in_d = nc.dram_tensor("ag_in", [MEM, MEM], F32, kind="Internal").ap()
    ag_out_d = nc.dram_tensor("ag_out", [NCORES * MEM, MEM], F32, kind="Internal").ap()

    dbg = {}
    if debug:
        for nm, shape, dt in (
            ("dbg_xlnT", [128, NHC * T], BF16), ("dbg_vtok", [128, NT * MEM], BF16),
            ("dbg_kT", [128, NMC * T], BF16), ("dbg_dmt", [128, NCH * 2 * MEM], F32),
            ("dbg_ac", [128, 2 * MEM], F32), ("dbg_mst", [128, NCH * 2 * MEM], F32),
            ("dbg_qT0", [128, NMC * 512], BF16), ("dbg_pT0", [128, 4 * 512], BF16),
            ("dbg_r", [128, NCH * 4 * MEM], BF16), ("dbg_x2", [128, NT * HID], F32),
            ("dbg_mask", [128, NT * NE], F32), ("dbg_xlnT2", [128, NHC * T], BF16),
            ("dbg_h1", [128, NFC * 1024], BF16),
        ):
            dbg[nm] = nc.dram_tensor(nm, shape, dt, kind="ExternalOutput").ap()

    DT_np, gsc_np = _np_consts()
    DT_d = nc.inline_tensor(np.asarray(DT_np, dtype=ml_dtypes.bfloat16), name="DTmask").ap()
    gsc_d = nc.inline_tensor(gsc_np, name="gscv").ap()
    idf_d = nc.inline_tensor(np.eye(128, dtype=np.float32), name="identf").ap()
    idb_d = nc.inline_tensor(np.eye(128, dtype=ml_dtypes.bfloat16), name="identb").ap()
    ones_d = nc.inline_tensor(np.ones((1, 128), dtype=ml_dtypes.bfloat16), name="onesrow").ap()

    stk = contextlib.ExitStack()
    sb = lambda name, shape, dt: stk.enter_context(nc.sbuf_tensor(name, shape, dt))

    with stk:
        xres = sb("xres", [128, NT * HID], F32)
        xlnT = sb("xlnT", [128, NHC * T], BF16)
        wr_s = sb("wr_s", [128, NHC * NE], BF16)
        b1r_s = sb("b1r_s", [128, NE * NFC], F32)
        br_s = sb("br_s", [1, NE], BF16)
        ones_s = sb("ones_s", [1, 128], BF16)
        rmax = sb("rmax", [128, 1], F32)
        rsum = sb("rsum", [128, 1], F32)
        rginv = sb("rginv", [128, 1], F32)
        rd8 = sb("rd8", [128, NE], F32)
        mask_s = sb("mask_s", [128, NT * NE], F32)
        w1buf = sb("w1buf", [128, 3 * 4096], BF16)
        b2buf = sb("b2buf", [1, HID], BF16)

        sems = {}
        for nm in ("LD", "GLD", "AGI", "CC", "AGL", "XL", "B2", "W1A", "W1B",
                   "W1C", "W2A", "W2B", "PE", "DV", "AC", "ST"):
            sems[nm] = _S(stk.enter_context(nc.semaphore(nm)))
        LD, GLD = sems["LD"], sems["GLD"]
        AGI, CC, AGL = sems["AGI"], sems["CC"], sems["AGL"]
        XL, B2 = sems["XL"], sems["B2"]
        W1S = (sems["W1A"], sems["W1B"], sems["W1C"])
        W2S = (sems["W2A"], sems["W2B"])
        PE, DV, AC, ST = sems["PE"], sems["DV"], sems["AC"], sems["ST"]

        free = {}

        def fset(key, sem):
            free[key] = (sem, sem.n)

        def fw(key):
            return [free[key]] if key in free else []

        pref = {}
        sq_guard = [0]

        def _emit_b2_load(e):
            P.em("gpsimd", lambda eng, e=e: eng.dma_start(
                out=b2buf[0:1, :], in_=b2_d[0:1, e * HID:(e + 1) * HID]),
                waits=fw(("b2buf",)), inc=B2, amt=16)
            return B2.n

        def _emit_w1_load(e, fq):
            par = fq % 3
            for hf in range(4):
                P.em("gpsimd", lambda eng, e=e, fq=fq, par=par, hf=hf: eng.dma_start(
                    out=w1buf[:, par * 4096 + hf * 1024:
                              par * 4096 + (hf + 1) * 1024].rearrange(
                        "p (hc j) -> p hc j", j=512),
                    in_=w1_d[e][hf * 256:(hf + 1) * 256,
                                fq * 512:(fq + 1) * 512].rearrange(
                        "(hc p) j -> p hc j", p=128)),
                    waits=(fw(("w1buf", par)) if hf == 0 else []),
                    inc=W1S[par], amt=16)
            return (W1S[par], W1S[par].n)

        with contextlib.ExitStack() as stkA:
            ec = stkA.enter_context
            identf = ec(nc.sbuf_tensor("identf_s", [128, 128], F32))
            gsc_s = ec(nc.sbuf_tensor("gsc_s", [128, 8], F32))
            xlnsc = ec(nc.sbuf_tensor("xlnsc", [128, 2 * HID], F32))
            sqdump = ec(nc.sbuf_tensor("sqdump", [128, HID], F32))
            lnsum = ec(nc.sbuf_tensor("lnsum", [128, NT], F32))
            lnss = ec(nc.sbuf_tensor("lnss", [128, NT], F32))
            lnm = ec(nc.sbuf_tensor("lnm", [128, NT], F32))
            lnvar = ec(nc.sbuf_tensor("lnvar", [128, NT], F32))
            lnsq = ec(nc.sbuf_tensor("lnsq", [128, NT], F32))
            lnrs = ec(nc.sbuf_tensor("lnrs", [128, NT], F32))
            lnt1 = ec(nc.sbuf_tensor("lnt1", [128, NT], F32))
            lnmm2 = ec(nc.sbuf_tensor("lnmm2", [128, NT], F32))
            wqT_s = ec(nc.sbuf_tensor("wqT_s", [128, NHC * MEM], BF16))
            wkT_s = ec(nc.sbuf_tensor("wkT_s", [128, NHC * MEM], BF16))
            wvT_s = ec(nc.sbuf_tensor("wvT_s", [128, NHC * MEM], BF16))
            woT_s = ec(nc.sbuf_tensor("woT_s", [128, NMC * HID], BF16))
            m0T_s = ec(nc.sbuf_tensor("m0T_s", [128, NMC * MEM], F32))
            coefs_s = ec(nc.sbuf_tensor("coefs_s", [128, 16], F32))
            gate_s = ec(nc.sbuf_tensor("gate_s", [128, 1], F32))
            gvec_s = ec(nc.sbuf_tensor("gvec_s", [128, 1], F32))
            identb = ec(nc.sbuf_tensor("identb_s", [128, 128], BF16))
            DT_s = ec(nc.sbuf_tensor("DT_s", [128, 4 * 512], BF16))
            vtok = ec(nc.sbuf_tensor("vtok", [128, NT * MEM], BF16))
            kT_s = ec(nc.sbuf_tensor("kT_s", [128, NMC * T], BF16))
            ktil = ec(nc.sbuf_tensor("ktil", [128, 4 * MEM], BF16))
            dmt = ec(nc.sbuf_tensor("dmt", [128, NCH * 2 * MEM], BF16))
            ac_s = ec(nc.sbuf_tensor("ac_s", [128, 2 * MEM], F32))
            agbuf = ec(nc.sbuf_tensor("agbuf", [128, 512], F32))
            mst = ec(nc.sbuf_tensor("mst", [128, NCH * 2 * MEM], F32))
            mbf = ec(nc.sbuf_tensor("mbf", [128, 2 * MEM], BF16))
            qT_s = ec(nc.sbuf_tensor("qT_s", [128, NMC * 512], BF16))
            pTt = ec(nc.sbuf_tensor("pTt", [128, 4 * 512], BF16))
            rtok = ec(nc.sbuf_tensor("rtok", [128, 4 * MEM], BF16))
            rT_s = ec(nc.sbuf_tensor("rT_s", [128, NMC * 512], BF16))
            psTf0 = ec(nc.psum_tensor("psTf0", [128, 128], F32))
            psTf1 = ec(nc.psum_tensor("psTf1", [128, 128], F32))
            psTb0 = ec(nc.psum_tensor("psTb0", [128, 128], BF16))
            psTb1 = ec(nc.psum_tensor("psTb1", [128, 128], BF16))
            psC = ec(nc.psum_tensor("psC", [128, MEM], F32))
            psD = ec(nc.psum_tensor("psD", [128, MEM], F32))
            psA = ec(nc.psum_tensor("psA", [128, 512], F32))
            psB = ec(nc.psum_tensor("psB", [128, 512], F32))
            blockA = ec(nc.Block())
            P = _Prog()

            def dma(dst, src):
                return lambda eng, d=dst, s=src: eng.dma_start(out=d, in_=s)

            P.em("sync", dma(identf[:, :], idf_d), inc=LD, amt=16)
            P.em("sync", dma(identb[:, :], idb_d), inc=LD, amt=16)
            P.em("sync", dma(ones_s[:, :], ones_d), inc=LD, amt=16)
            P.em("sync", dma(DT_s[:].rearrange("p (r i) -> p r i", i=512),
                             DT_d.rearrange("(r p) i -> p r i", p=128)), inc=LD, amt=16)
            P.em("sync", dma(gsc_s[:, :], gsc_d), inc=LD, amt=16)
            P.em("sync", dma(b1r_s[:, :], b1r_d), inc=LD, amt=16)
            P.em("sync", dma(coefs_s[:, :], coefs_d), inc=LD, amt=16)
            P.em("sync", dma(m0T_s[:].rearrange("p (mc m) -> p mc m", m=MEM),
                             m0T_d.rearrange("(mc p) m -> p mc m", p=128)), inc=LD, amt=16)
            P.em("sync", dma(gate_s[:, :], gate_d), inc=LD, amt=16)
            gate_ld = LD.n
            P.em("sync", dma(xres[:].rearrange("p (t j) -> p t j", j=HID),
                             x_d.rearrange("(t p) j -> p t j", p=128)), inc=XL, amt=16)
            LD_ALL = LD.n

            for dst, src in ((wqT_s, wqT_d), (wkT_s, wkT_d), (wvT_s, wvT_d)):
                P.em("gpsimd", dma(dst[:].rearrange("p (hc m) -> p hc m", m=MEM),
                                   src.rearrange("(hc p) m -> p hc m", p=128)), inc=GLD, amt=16)
            P.em("gpsimd", dma(woT_s[:].rearrange("p (mc h) -> p mc h", h=HID),
                               woT_d.rearrange("(mc p) h -> p mc h", p=128)), inc=GLD, amt=16)
            P.em("gpsimd", dma(wr_s[:].rearrange("p (hc e) -> p hc e", e=NE),
                               wr_d.rearrange("(hc p) e -> p hc e", p=128)), inc=GLD, amt=16)
            P.em("gpsimd", dma(br_s[:, :], br_d), inc=GLD, amt=16)
            GLD_W = GLD.n

            P.em("scalar", lambda eng: eng.activation(gvec_s[:, :], gate_s[:, :], AF.Sigmoid),
                 waits=[(LD, gate_ld)], inc=AC)
            gv_ready = AC.n

            def emit_ln_stats_group(tiles, waits_red=(), waits_sq=()):
                for i, t in enumerate(tiles):
                    P.em("vector", lambda eng, t=t: eng.reduce_sum(
                        lnsum[:, t:t + 1], xres[:, t * HID:(t + 1) * HID], axis=AX.X),
                        waits=(list(waits_red) if i == 0 else []),
                        inc=(DV if i == len(tiles) - 1 else None))
                    P.em("scalar", lambda eng, t=t: eng.activation(
                        sqdump[:, :], xres[:, t * HID:(t + 1) * HID], AF.Square,
                        accum_out=lnss[:, t:t + 1]),
                        waits=(list(waits_sq) if i == 0 else [(AC, AC.n)]),
                        inc=AC)
                sq_guard[0] = AC.n
                return DV.n, AC.n

            def emit_ln_chain(t0, n, red_done, sq_done):
                c = slice(t0, t0 + n)
                P.em("vector", lambda eng: eng.tensor_scalar(
                    lnm[:, c], lnsum[:, c], 1.0 / HID, None, OP.mult),
                    waits=[(DV, red_done), (AC, sq_done)], inc=DV)
                P.em("vector", lambda eng: eng.tensor_mul(lnt1[:, c], lnm[:, c], lnm[:, c]),
                     waits=[(DV, DV.n)], inc=DV)
                P.em("vector", lambda eng: eng.tensor_scalar(
                    lnvar[:, c], lnss[:, c], 1.0 / HID, 1e-5, OP.mult, OP.add), inc=DV)
                P.em("vector", lambda eng: eng.tensor_sub(lnvar[:, c], lnvar[:, c], lnt1[:, c]),
                     waits=[(DV, DV.n)], inc=DV)
                var_ready = DV.n
                P.em("scalar", lambda eng: eng.activation(lnsq[:, c], lnvar[:, c], AF.Sqrt),
                     waits=[(DV, var_ready)], inc=AC)
                sq2 = AC.n
                P.em("vector", lambda eng: eng.reciprocal(lnrs[:, c], lnsq[:, c]),
                     waits=[(AC, sq2)], inc=DV)
                P.em("vector", lambda eng: eng.tensor_mul(lnt1[:, c], lnrs[:, c], lnrs[:, c]),
                     waits=[(DV, DV.n)], inc=DV)
                P.em("vector", lambda eng: eng.tensor_mul(lnt1[:, c], lnt1[:, c], lnvar[:, c]),
                     waits=[(DV, DV.n)], inc=DV)
                P.em("vector", lambda eng: eng.tensor_scalar(
                    lnt1[:, c], lnt1[:, c], -0.5, 1.5, OP.mult, OP.add),
                    waits=[(DV, DV.n)], inc=DV)
                P.em("vector", lambda eng: eng.tensor_mul(lnrs[:, c], lnrs[:, c], lnt1[:, c]),
                     waits=[(DV, DV.n)], inc=DV)
                P.em("vector", lambda eng: eng.tensor_mul(lnmm2[:, c], lnm[:, c], lnrs[:, c]),
                     waits=[(DV, DV.n)], inc=DV)
                P.em("vector", lambda eng: eng.tensor_scalar(
                    lnmm2[:, c], lnmm2[:, c], -1.0, None, OP.mult),
                    waits=[(DV, DV.n)], inc=DV)
                return DV.n

            def emit_ln_tile(t, ps_slices, ps_keys):
                par = t % 2
                w = fw(("xlnsc", par)) + [(DV, DV.n)]
                P.em("vector", lambda eng, t=t, par=par: eng.tensor_scalar(
                    xlnsc[:, par * HID:(par + 1) * HID], xres[:, t * HID:(t + 1) * HID],
                    lnrs[:, t:t + 1], lnmm2[:, t:t + 1], OP.mult, OP.add),
                    waits=w, inc=DV)
                dv_xln = DV.n
                for hc in range(NHC):
                    bank = hc % 2
                    w = ([(DV, dv_xln)] if hc == 0 else []) + fw(ps_keys[bank])
                    P.em("tensor", lambda eng, hc=hc, bank=bank, par=par: eng.transpose(
                        ps_slices[bank],
                        xlnsc[:, par * HID + hc * 128: par * HID + (hc + 1) * 128],
                        identf[:, :]),
                        waits=w, inc=PE)
                    tp = PE.n
                    if hc == NHC - 1:
                        fset(("xlnsc", par), PE)
                    P.em("vector", lambda eng, hc=hc, bank=bank, t=t: eng.tensor_copy(
                        xlnT[:, hc * T + t * 128: hc * T + (t + 1) * 128], ps_slices[bank]),
                        waits=[(PE, tp)], inc=DV)
                    fset(ps_keys[bank], DV)
                return DV.n

            rd, sd = emit_ln_stats_group(
                range(NT), [(XL, 16), (LD, LD_ALL)], [(XL, 16), (LD, LD_ALL)])
            emit_ln_chain(0, NT, rd, sd)
            tile_evac = [0] * NT
            tile2_evac = [0] * NT
            for t in range(NT):
                tile_evac[t] = emit_ln_tile(t, (psTf0[:, :], psTf1[:, :]),
                                            ("psTf0", "psTf1"))

                w = [(DV, tile_evac[t]), (GLD, GLD_W)] + fw("psC")
                for hc in range(NHC):
                    P.em("tensor", lambda eng, hc=hc, t=t: eng.matmul(
                        psC[:, :],
                        xlnT[:, hc * T + t * 128: hc * T + (t + 1) * 128],
                        wvT_s[:, hc * MEM:(hc + 1) * MEM],
                        start=(hc == 0), stop=(hc == NHC - 1)),
                        waits=(w if hc == 0 else []), inc=(PE if hc == NHC - 1 else None))
                vmm = PE.n
                P.em("vector", lambda eng, t=t: eng.tensor_copy(
                    vtok[:, t * MEM:(t + 1) * MEM], psC[:, :]), waits=[(PE, vmm)], inc=DV)
                fset("psC", DV)

                w = fw("psD")
                for hc in range(NHC):
                    P.em("tensor", lambda eng, hc=hc, t=t: eng.matmul(
                        psD[:, :],
                        xlnT[:, hc * T + t * 128: hc * T + (t + 1) * 128],
                        wkT_s[:, hc * MEM:(hc + 1) * MEM],
                        start=(hc == 0), stop=(hc == NHC - 1)),
                        waits=(w if hc == 0 else []), inc=(PE if hc == NHC - 1 else None))
                kmm = PE.n
                rt = t % 4
                P.em("vector", lambda eng, rt=rt: eng.tensor_scalar(
                    ktil[:, rt * MEM:(rt + 1) * MEM], psD[:, :], gsc_s[:, rt:rt + 1], None,
                    OP.mult), waits=[(PE, kmm)], inc=DV)
                fset("psD", DV)
                ktil_ready = DV.n

                if rt == 3:
                    h = t // 4
                    for mc in range(NMC):
                        ps = psC if mc == 0 else psD
                        pskey = "psC" if mc == 0 else "psD"
                        w = [(DV, ktil_ready)] + fw(pskey)
                        for rr in range(4):
                            P.em("tensor", lambda eng, rr=rr, mc=mc, h=h, ps=ps: eng.matmul(
                                ps[:, :],
                                ktil[:, rr * MEM + mc * 128: rr * MEM + (mc + 1) * 128],
                                vtok[:, (4 * h + rr) * MEM:(4 * h + rr + 1) * MEM],
                                start=(rr == 0), stop=(rr == 3)),
                                waits=(w if rr == 0 else []), inc=(PE if rr == 3 else None))
                        dmm = PE.n
                        P.em("vector", lambda eng, h=h, mc=mc, ps=ps: eng.tensor_copy(
                            dmt[:, (h * 2 + mc) * MEM:(h * 2 + mc + 1) * MEM], ps[:, :]),
                            waits=[(PE, dmm)], inc=DV)
                        fset(pskey, DV)

                    for mc in range(NMC):
                        ps = psA if mc == 0 else psB
                        pskey = "psA" if mc == 0 else "psB"
                        w = [(DV, tile_evac[t])] + fw(pskey)
                        for hc in range(NHC):
                            P.em("tensor", lambda eng, hc=hc, mc=mc, h=h, ps=ps: eng.matmul(
                                ps[:, :],
                                wkT_s[:, hc * MEM + mc * 128: hc * MEM + (mc + 1) * 128],
                                xlnT[:, hc * T + h * 512: hc * T + (h + 1) * 512],
                                start=(hc == 0), stop=(hc == NHC - 1)),
                                waits=(w if hc == 0 else []), inc=(PE if hc == NHC - 1 else None))
                        kTmm = PE.n
                        P.em("vector", lambda eng, mc=mc, h=h, ps=ps: eng.tensor_copy(
                            kT_s[:, mc * T + h * 512: mc * T + (h + 1) * 512], ps[:, :]),
                            waits=[(PE, kTmm)], inc=DV)
                        fset(pskey, DV)

            P.em("vector", lambda eng: eng.scalar_tensor_tensor(
                ac_s[:, :], dmt[:, 0:512], G128, dmt[:, 512:1024], OP.mult, OP.add),
                waits=[(DV, DV.n)], inc=DV)
            P.em("vector", lambda eng: eng.scalar_tensor_tensor(
                ac_s[:, :], ac_s[:, :], G128, dmt[:, 1024:1536], OP.mult, OP.add),
                waits=[(DV, DV.n)], inc=DV)
            P.em("vector", lambda eng: eng.scalar_tensor_tensor(
                ac_s[:, :], ac_s[:, :], G128, dmt[:, 1536:2048], OP.mult, OP.add),
                waits=[(DV, DV.n)], inc=DV)
            ac_ready = DV.n

            if debug:
                for nm, src in (("dbg_xlnT", xlnT), ("dbg_vtok", vtok), ("dbg_kT", kT_s),
                                ("dbg_dmt", dmt), ("dbg_ac", ac_s)):
                    P.em("scalar", dma(dbg[nm], src[:, :]), waits=[(DV, ac_ready)],
                         inc=ST, amt=16)

            P.em("gpsimd", dma(ag_in_d.rearrange("(mc p) m -> p mc m", p=128),
                               ac_s[:].rearrange("p (mc m) -> p mc m", m=MEM)),
                 waits=[(DV, ac_ready)], inc=AGI, amt=16)
            P.em("gpsimd", lambda eng: eng.collective_compute(
                "AllGather", OP.bypass, replica_groups=[list(range(NCORES))],
                ins=[ag_in_d.opt()], outs=[ag_out_d.opt()]),
                waits=[(AGI, AGI.n)], inc=CC, amt=1)
            cc_done = CC.n
            for r in range(NCORES):
                w = ([(CC, cc_done)] if r == 0 else []) + fw(("agbuf",))
                P.em("gpsimd", dma(
                    agbuf[:, :].rearrange("p (mc m) -> p mc m", m=MEM),
                    ag_out_d[r * MEM:(r + 1) * MEM, :].rearrange("(mc p) m -> p mc m", p=128)),
                    waits=w, inc=AGL, amt=16)
                rank_ld = AGL.n
                if r == 0:
                    P.em("vector", lambda eng: eng.tensor_scalar(
                        mst[:, 0:512], agbuf[:, :], coefs_s[:, 0:1], None, OP.mult),
                        waits=[(AGL, rank_ld), (LD, LD_ALL)], inc=DV)
                else:
                    P.em("vector", lambda eng, r=r: eng.scalar_tensor_tensor(
                        mst[:, 0:512], agbuf[:, :], coefs_s[:, r:r + 1],
                        mst[:, 0:512], OP.mult, OP.add),
                        waits=[(AGL, rank_ld), (DV, DV.n)], inc=DV)
                fset(("agbuf",), DV)
            P.em("vector", lambda eng: eng.scalar_tensor_tensor(
                mst[:, 0:512], m0T_s[:, :], coefs_s[:, 8:9], mst[:, 0:512], OP.mult, OP.add),
                waits=[(DV, DV.n)], inc=DV)
            for hh in range(1, NCH):
                P.em("vector", lambda eng, hh=hh: eng.scalar_tensor_tensor(
                    mst[:, hh * 512:(hh + 1) * 512], mst[:, (hh - 1) * 512:hh * 512], G128,
                    dmt[:, (hh - 1) * 512:hh * 512], OP.mult, OP.add),
                    waits=[(DV, DV.n)], inc=DV)
            mst_ready = DV.n

            if debug:
                P.em("scalar", dma(dbg["dbg_mst"], mst[:, :]), waits=[(DV, mst_ready)],
                     inc=ST, amt=16)

            pref["id"] = (0, 0)
            pref["b2"] = _emit_b2_load(0)
            pref["w1"] = {fq0: _emit_w1_load(0, fq0) for fq0 in range(3)}

            dbg_st_gate = 0
            for h in range(NCH):
                P.em("vector", lambda eng, h=h: eng.tensor_copy(
                    mbf[:, :], mst[:, h * 512:(h + 1) * 512]),
                    waits=[(DV, mst_ready)] + fw(("mbf",)), inc=DV)
                mbf_ready = DV.n
                if debug and h > 0:
                    P.em("vector", lambda eng: eng.engine_nop(), waits=[(ST, dbg_st_gate)])
                qT_ready = [0, 0]
                for mc in range(NMC):
                    ps = psA if mc == 0 else psB
                    pskey = "psA" if mc == 0 else "psB"
                    w = fw(pskey) + [(GLD, GLD_W)]
                    for hc in range(NHC):
                        P.em("tensor", lambda eng, hc=hc, mc=mc, h=h, ps=ps: eng.matmul(
                            ps[:, :],
                            wqT_s[:, hc * MEM + mc * 128: hc * MEM + (mc + 1) * 128],
                            xlnT[:, hc * T + h * 512: hc * T + (h + 1) * 512],
                            start=(hc == 0), stop=(hc == NHC - 1)),
                            waits=(w if hc == 0 else []), inc=(PE if hc == NHC - 1 else None))
                    qmm = PE.n
                    P.em("vector", lambda eng, mc=mc, ps=ps: eng.tensor_copy(
                        qT_s[:, mc * 512:(mc + 1) * 512], ps[:, :]), waits=[(PE, qmm)], inc=DV)
                    fset(pskey, DV)
                    qT_ready[mc] = DV.n

                pt_ready = 0
                for rkt in range(4):
                    ps = psA if rkt % 2 == 0 else psB
                    pskey = "psA" if rkt % 2 == 0 else "psB"
                    w = [(DV, max(qT_ready))] + fw(pskey)
                    for mc in range(NMC):
                        P.em("tensor", lambda eng, rkt=rkt, mc=mc, h=h, ps=ps: eng.matmul(
                            ps[:, :],
                            kT_s[:, mc * T + (4 * h + rkt) * 128: mc * T + (4 * h + rkt + 1) * 128],
                            qT_s[:, mc * 512:(mc + 1) * 512],
                            start=(mc == 0), stop=(mc == NMC - 1)),
                            waits=(w if mc == 0 else []), inc=(PE if mc == NMC - 1 else None))
                    smm = PE.n
                    P.em("vector", lambda eng, rkt=rkt, ps=ps: eng.tensor_tensor(
                        pTt[:, rkt * 512:(rkt + 1) * 512], ps[:, :],
                        DT_s[:, rkt * 512:(rkt + 1) * 512], OP.mult),
                        waits=[(PE, smm)], inc=DV)
                    fset(pskey, DV)
                    pt_ready = DV.n

                for rt in range(4):
                    w = [(DV, pt_ready)] + fw("psC")
                    for rkt in range(4):
                        P.em("tensor", lambda eng, rkt=rkt, rt=rt, h=h: eng.matmul(
                            psC[:, :],
                            pTt[:, rkt * 512 + rt * 128: rkt * 512 + (rt + 1) * 128],
                            vtok[:, (4 * h + rkt) * MEM:(4 * h + rkt + 1) * MEM],
                            start=(rkt == 0), stop=(rkt == 3)),
                            waits=(w if rkt == 0 else []), inc=(PE if rkt == 3 else None))
                    intra_mm = PE.n
                    w = [(DV, mbf_ready)] + fw("psD")
                    for mc in range(NMC):
                        P.em("tensor", lambda eng, mc=mc, rt=rt, h=h: eng.matmul(
                            psD[:, :],
                            qT_s[:, mc * 512 + rt * 128: mc * 512 + (rt + 1) * 128],
                            mbf[:, mc * MEM:(mc + 1) * MEM],
                            start=(mc == 0), stop=(mc == NMC - 1)),
                            waits=(w if mc == 0 else []), inc=(PE if mc == NMC - 1 else None))
                    inter_mm = PE.n
                    if rt == 3:
                        fset(("mbf",), PE)
                    P.em("vector", lambda eng: eng.tensor_copy(sqdump[:, 0:MEM], psC[:, :]),
                         waits=[(PE, intra_mm), (AC, sq_guard[0])], inc=DV)
                    fset("psC", DV)
                    P.em("vector", lambda eng, rt=rt: eng.scalar_tensor_tensor(
                        rtok[:, rt * MEM:(rt + 1) * MEM], psD[:, :], gsc_s[:, 4 + rt:5 + rt],
                        sqdump[:, 0:MEM], OP.mult, OP.add),
                        waits=[(PE, inter_mm), (DV, DV.n)], inc=DV)
                    fset("psD", DV)
                    r_ready = DV.n

                    for mc in range(NMC):
                        bank = mc % 2
                        psb = psTb0 if bank == 0 else psTb1
                        pskey = "psTb0" if bank == 0 else "psTb1"
                        w = [(DV, r_ready)] + fw(pskey)
                        P.em("tensor", lambda eng, mc=mc, rt=rt, psb=psb: eng.transpose(
                            psb[:, :], rtok[:, rt * MEM + mc * 128: rt * MEM + (mc + 1) * 128],
                            identb[:, :]), waits=w, inc=PE)
                        tmm = PE.n
                        P.em("vector", lambda eng, mc=mc, rt=rt, psb=psb: eng.tensor_copy(
                            rT_s[:, mc * 512 + rt * 128: mc * 512 + (rt + 1) * 128], psb[:, :]),
                            waits=[(PE, tmm)], inc=DV)
                        fset(pskey, DV)
                    rT_ready = DV.n

                    for hg in range(2):
                        ps = psA if hg == 0 else psB
                        pskey = "psA" if hg == 0 else "psB"
                        w = [(DV, rT_ready)] + fw(pskey)
                        for mc in range(NMC):
                            P.em("tensor", lambda eng, mc=mc, rt=rt, hg=hg, ps=ps: eng.matmul(
                                ps[:, :],
                                rT_s[:, mc * 512 + rt * 128: mc * 512 + (rt + 1) * 128],
                                woT_s[:, mc * HID + hg * 512: mc * HID + (hg + 1) * 512],
                                start=(mc == 0), stop=(mc == NMC - 1)),
                                waits=(w if mc == 0 else []), inc=(PE if mc == NMC - 1 else None))
                        omm = PE.n
                        tt = 4 * h + rt
                        P.em("vector", lambda eng, tt=tt, hg=hg, ps=ps: eng.scalar_tensor_tensor(
                            xres[:, tt * HID + hg * 512: tt * HID + (hg + 1) * 512],
                            ps[:, :], gvec_s[:, :],
                            xres[:, tt * HID + hg * 512: tt * HID + (hg + 1) * 512],
                            OP.mult, OP.add),
                            waits=[(PE, omm), (AC, gv_ready)], inc=DV)
                        fset(pskey, DV)

                if debug:
                    if h == 0:
                        P.em("scalar", dma(dbg["dbg_qT0"], qT_s[:, :]),
                             waits=[(DV, DV.n)], inc=ST, amt=16)
                        P.em("scalar", dma(dbg["dbg_pT0"], pTt[:, :]),
                             waits=[(DV, DV.n)], inc=ST, amt=16)
                    P.em("scalar", lambda eng, h=h: eng.dma_start(
                        out=dbg["dbg_r"][:, h * 1024:(h + 1) * 1024], in_=rtok[:, :]),
                        waits=[(DV, DV.n)], inc=ST, amt=16)
                    dbg_st_gate = ST.n

                resid_dv = DV.n
                tl = [4 * h + i for i in range(4)]
                rd2, sd2 = emit_ln_stats_group(tl, [(DV, resid_dv)], [(DV, resid_dv)])
                emit_ln_chain(4 * h, 4, rd2, sd2)
                for t in tl:
                    tile2_evac[t] = emit_ln_tile(t, (psTf0[:, :], psTf1[:, :]),
                                                 ("psTf0", "psTf1"))

            phaseA_pe_done = PE.n
            phaseA_dv_done = DV.n
            phaseA_ac_done = AC.n
            dbg_st_A = ST.n
            P.replay(blockA)

        TGT = 1024
        NTG = T // TGT
        with contextlib.ExitStack() as stkB:
            ec = stkB.enter_context
            h1sb = ec(nc.sbuf_tensor("h1sb", [128, NFC * TGT], BF16))
            w2buf = ec(nc.sbuf_tensor("w2buf", [128, 2 * 2048], BF16))
            psFF = ec(nc.psum_tensor("psFF", [128, 4096], F32))
            blockB = ec(nc.Block())
            P = _Prog()

            def dma(dst, src):
                return lambda eng, d=dst, s=src: eng.dma_start(out=d, in_=s)

            pb_guard = [(DV, phaseA_dv_done), (AC, phaseA_ac_done)]
            if debug:
                P.em("scalar", dma(dbg["dbg_x2"], xres[:, :]),
                     waits=[(DV, phaseA_dv_done)], inc=ST, amt=16)
                pb_guard = pb_guard + [(ST, ST.n)]

            def emit_router_all():
                base = 2560
                w0 = [(DV, phaseA_dv_done), (GLD, GLD_W), (DV, tile2_evac[NT - 1])]
                for t in range(NT):
                    rps = psFF[:, base + t * NE: base + (t + 1) * NE]
                    for hc in range(NHC):
                        P.em("tensor", lambda eng, hc=hc, t=t, rps=rps: eng.matmul(
                            rps, xlnT[:, hc * T + t * 128: hc * T + (t + 1) * 128],
                            wr_s[:, hc * NE:(hc + 1) * NE],
                            start=(hc == 0), stop=False),
                            waits=(w0 if (t == 0 and hc == 0) else []))
                    P.em("tensor", lambda eng, rps=rps: eng.matmul(
                        rps, ones_s[:, :], br_s[:, :], start=False, stop=True),
                        inc=(PE if t == NT - 1 else None))
                lg_done = PE.n
                for t in range(NT):
                    rps = psFF[:, base + t * NE: base + (t + 1) * NE]
                    P.em("vector", lambda eng, rps=rps: eng.reduce_max(
                        rmax[:, :], rps, axis=AX.X, negate=True),
                        waits=([(PE, lg_done)] if t == 0 else [(DV, DV.n), (AC, AC.n)]),
                        inc=DV)
                    negmax_ready = DV.n
                    P.em("scalar", lambda eng, rps=rps: eng.activation(
                        rd8[:, :], rps, AF.Exp, bias=rmax[:, :], accum_out=rsum[:, :]),
                        waits=[(DV, negmax_ready)] + ([(AC, AC.n)] if t > 0 else []),
                        inc=AC)
                    exp_ready = AC.n
                    P.em("vector", lambda eng: eng.reciprocal(rginv[:, :], rsum[:, :]),
                         waits=[(AC, exp_ready)], inc=DV)
                    P.em("vector", lambda eng, rps=rps: eng.tensor_scalar(
                        rd8[:, :], rps, rmax[:, :], None, OP.add),
                        waits=[(DV, DV.n)], inc=DV)
                    P.em("vector", lambda eng, t=t: eng.tensor_scalar(
                        mask_s[:, t * NE:(t + 1) * NE], rd8[:, :], 0.0, rginv[:, :],
                        OP.is_equal, OP.mult), waits=[(DV, DV.n)], inc=DV)

            first_w2load = [True]

            def emit_ffn_expert(tg, e, y_guard=None, nxt=None):
                xln_tg = tile2_evac[8 * tg + 7]
                w1_rec = [0] * 8
                if pref.get("id") == (tg, e):
                    b2_ld = pref["b2"]
                    for fq0 in range(3):
                        w1_rec[fq0] = pref["w1"][fq0]
                    pref.clear()
                else:
                    b2_ld = _emit_b2_load(e)
                    for fq0 in range(3):
                        w1_rec[fq0] = _emit_w1_load(e, fq0)

                def w1_load(fq):
                    w1_rec[fq] = _emit_w1_load(e, fq)

                h1_done = 0
                for fq in range(8):
                    par = fq % 3
                    for fc4 in range(4):
                        fc = fq * 4 + fc4
                        slot = fc % 2
                        skey = ("psffh", slot)
                        w = ([w1_rec[fq], (DV, xln_tg)] if fc4 == 0 else [])
                        w += fw(skey)
                        if fq == 0 and fc4 < 2:
                            w += fw(("psffYb", 2 * slot)) + fw(("psffYb", 2 * slot + 1))
                        for hc in range(NHC):
                            for th in range(2):
                                P.em("tensor", lambda eng, hc=hc, fc4=fc4, th=th, par=par,
                                     tg=tg, slot=slot: eng.matmul(
                                    psFF[:, slot * 1024 + th * 512: slot * 1024 + (th + 1) * 512],
                                    w1buf[:, par * 4096 + hc * 512 + fc4 * 128:
                                          par * 4096 + hc * 512 + (fc4 + 1) * 128],
                                    xlnT[:, hc * T + tg * TGT + th * 512:
                                         hc * T + tg * TGT + (th + 1) * 512],
                                    start=(hc == 0), stop=(hc == NHC - 1),
                                    skip_group_check=True),
                                    waits=(w if (hc == 0 and th == 0) else []),
                                    inc=(PE if (hc == NHC - 1 and th == 1) else None))
                        hmm_ = PE.n
                        if fc4 == 3:
                            free[("w1buf", par)] = (PE, PE.n)
                        P.em("scalar", lambda eng, fc=fc, e=e, slot=slot: eng.activation(
                            h1sb[:, fc * TGT:(fc + 1) * TGT],
                            psFF[:, slot * 1024:(slot + 1) * 1024],
                            AF.Relu, bias=b1r_s[:, e * NFC + fc: e * NFC + fc + 1]),
                            waits=[(PE, hmm_)], inc=AC)
                        free[("psffh", slot)] = (AC, AC.n)
                        h1_done = AC.n
                    if fq + 3 < 8:
                        w1_load(fq + 3)

                if debug and tg == 0 and e == 0:
                    P.em("scalar", dma(dbg["dbg_h1"], h1sb[:, :]),
                         waits=[(AC, h1_done)], inc=ST, amt=16)
                    P.em("scalar", lambda eng: eng.nop(), waits=[(ST, ST.n)])

                w2_pairs = [(g, fq) for g in range(2) for fq in range(8)]
                w2_rec = {}

                def w2_load(idx):
                    g, fq = w2_pairs[idx]
                    par = idx % 2
                    key = ("w2buf", par)
                    if first_w2load[0]:
                        free[key] = (DV, phaseA_dv_done)
                        first_w2load[0] = False
                    for hf in range(4):
                        P.em("gpsimd", dma(
                            w2buf[:, par * 2048 + hf * 512:
                                  par * 2048 + (hf + 1) * 512].rearrange(
                                "p (f j) -> p f j", j=512),
                            w2_d[e][fq * 512 + hf * 128:fq * 512 + (hf + 1) * 128,
                                    g * 512:(g + 1) * 512].rearrange(
                                "(f p) j -> p f j", p=128)),
                            waits=(fw(key) if hf == 0 else []),
                            inc=W2S[par], amt=16)
                    w2_rec[(g, fq)] = (W2S[par], W2S[par].n)

                w2_load(0)
                w2_load(1)
                for g in range(2):
                    w = [(B2, b2_ld)]
                    if y_guard is not None and g == 0:
                        w += [(DV, y_guard[0]), (AC, y_guard[1])]
                    for t8 in range(8):
                        P.em("tensor", lambda eng, t8=t8, g=g: eng.matmul(
                            psFF[:, t8 * 512:(t8 + 1) * 512], ones_s[:, :],
                            b2buf[0:1, g * 512:(g + 1) * 512],
                            start=True, stop=False, skip_group_check=True),
                            waits=((w if t8 == 0 else []) + [(AC, h1_done)]
                                   + fw(("psffYb", t8))))
                    for fq in range(8):
                        idx = g * 8 + fq
                        par = idx % 2
                        w = [w2_rec[(g, fq)]]
                        for fc4 in range(4):
                            fc = fq * 4 + fc4
                            for t8 in range(8):
                                last = (fc == NFC - 1)
                                P.em("tensor", lambda eng, fc=fc, fc4=fc4, t8=t8, par=par,
                                     g=g, last=last: eng.matmul(
                                    psFF[:, t8 * 512:(t8 + 1) * 512],
                                    h1sb[:, fc * TGT + t8 * 128: fc * TGT + (t8 + 1) * 128],
                                    w2buf[:, par * 2048 + fc4 * 512:
                                          par * 2048 + (fc4 + 1) * 512],
                                    start=False, stop=last, skip_group_check=True),
                                    waits=(w if (fc4 == 0 and t8 == 0) else []),
                                    inc=(PE if (fc4 == 3 and t8 == 7) else None))
                        free[("w2buf", par)] = (PE, PE.n)
                        if idx + 2 < 16:
                            w2_load(idx + 2)
                        if g == 1 and fq == 0 and nxt is not None:
                            free[("b2buf",)] = (PE, PE.n)
                            pref["id"] = nxt
                            pref["b2"] = _emit_b2_load(nxt[1])
                            pref["w1"] = {fq0: _emit_w1_load(nxt[1], fq0)
                                          for fq0 in range(3)}
                    if g == 1:
                        free[("b2buf",)] = (PE, PE.n)
                    y_done = PE.n

                    for t8 in range(8):
                        tt = 8 * tg + t8
                        P.em("vector", lambda eng, tt=tt, t8=t8, g=g, e=e:
                             eng.scalar_tensor_tensor(
                                 xres[:, tt * HID + g * 512: tt * HID + (g + 1) * 512],
                                 psFF[:, t8 * 512:(t8 + 1) * 512],
                                 mask_s[:, tt * NE + e: tt * NE + e + 1],
                                 xres[:, tt * HID + g * 512: tt * HID + (g + 1) * 512],
                                 OP.mult, OP.add),
                             waits=[(PE, y_done)], inc=DV)
                        free[("psffYb", t8)] = (DV, DV.n)

            def emit_store(tg):
                st_w = [(DV, DV.n)]
                P.em("scalar", lambda eng, tg=tg: eng.dma_start(
                    out=y_d[tg * TGT:(tg + 1) * TGT, :].rearrange("(t p) j -> p t j", p=128),
                    in_=xres[:, tg * 8 * HID:(tg + 1) * 8 * HID].rearrange(
                        "p (t j) -> p t j", j=HID)),
                    waits=st_w, inc=ST, amt=16)

            emit_router_all()
            router_dv_done, router_ac_done = DV.n, AC.n
            if debug:
                P.em("scalar", dma(dbg["dbg_mask"], mask_s[:, :]),
                     waits=[(DV, router_dv_done)], inc=ST, amt=16)
                P.em("scalar", dma(dbg["dbg_xlnT2"], xlnT[:, :]),
                     waits=[(DV, router_dv_done)], inc=ST, amt=16)
            order = [(tg, e) for tg in range(NTG) for e in range(NE)]
            for i, (tg, e) in enumerate(order):
                emit_ffn_expert(tg, e,
                                y_guard=(router_dv_done, router_ac_done)
                                if (tg == 0 and e == 0) else None,
                                nxt=(order[i + 1] if i + 1 < len(order) else None))
                if e == NE - 1:
                    emit_store(tg)

            P.em("scalar", lambda eng: eng.nop(), waits=[(ST, ST.n)])
            P.replay(blockB)

    return nc


_NC_CACHE = {}


def kernel(**inputs):
    x = np.asarray(inputs["L0_attn_x"], dtype=np.float32)
    Wq = np.asarray(inputs["L0_attn_Wq"], dtype=np.float32)
    Wk = np.asarray(inputs["L0_attn_Wk"], dtype=np.float32)
    Wv = np.asarray(inputs["L0_attn_Wv"], dtype=np.float32)
    Wo = np.asarray(inputs["L0_attn_Wo"], dtype=np.float32)
    gate = np.asarray(inputs["L0_attn_gate"], dtype=np.float32)
    M0 = np.asarray(inputs["L0_attn_M0"], dtype=np.float32)
    Wr = np.asarray(inputs["L0_ffn_Wr"], dtype=np.float32)
    br = np.asarray(inputs["L0_ffn_br"], dtype=np.float32)
    W1 = np.asarray(inputs["L0_ffn_W1"], dtype=np.float32)
    b1 = np.asarray(inputs["L0_ffn_b1"], dtype=np.float32)
    W2 = np.asarray(inputs["L0_ffn_W2"], dtype=np.float32)
    b2 = np.asarray(inputs["L0_ffn_b2"], dtype=np.float32)

    if "nc" not in _NC_CACHE:
        _NC_CACHE["nc"] = build_nc()
    nc = _NC_CACHE["nc"]

    wqT = np.ascontiguousarray(Wq.T)
    wkT = np.ascontiguousarray(Wk.T)
    wvT = np.ascontiguousarray(Wv.T)
    woT = np.ascontiguousarray(Wo.T)
    m0T = np.ascontiguousarray(M0.T)
    gate128 = np.ascontiguousarray(np.tile(gate.reshape(1, 1), (128, 1)))
    b1r = np.ascontiguousarray(b1.reshape(NE, NFC, 128).transpose(2, 0, 1).reshape(128, NE * NFC))
    b2row = np.ascontiguousarray(b2.reshape(1, NE * HID))
    brr = np.ascontiguousarray(br.reshape(1, NE))
    w1c = np.ascontiguousarray(W1)
    w2c = np.ascontiguousarray(W2)

    in_maps = []
    for c in range(NCORES):
        xc = np.ascontiguousarray(
            x[:, c * SC:(c + 1) * SC, :].transpose(1, 0, 2).reshape(T, HID))
        coefs = np.zeros((16,), dtype=np.float64)
        for cp in range(c):
            coefs[cp] = GAMMA ** (SC * (c - 1 - cp))
        coefs[8] = GAMMA ** (SC * c)
        coefs128 = np.ascontiguousarray(
            np.tile(coefs.astype(np.float32).reshape(1, 16), (128, 1)))
        in_maps.append({
            "x": xc, "wqT": wqT, "wkT": wkT, "wvT": wvT, "woT": woT,
            "m0T": m0T, "gate128": gate128, "wr": Wr, "br": brr,
            "b1r": b1r, "b2row": b2row, "w1": w1c, "w2": w2c,
            "coefs": coefs128,
        })

    res = run_bass_kernel_spmd(nc, in_maps, core_ids=list(range(NCORES)))
    out = np.empty((BATCH, SEQ, HID), dtype=np.float32)
    for c in range(NCORES):
        shard = np.asarray(res.results[c]["y"])
        out[:, c * SC:(c + 1) * SC, :] = shard.reshape(SC, BATCH, HID).transpose(1, 0, 2)
    return out


# version 17 (final)
# speedup vs baseline: 1.0551x; 1.0551x over previous
import contextlib
import numpy as np
import ml_dtypes
import concourse.bass as bass
import concourse.mybir as mybir
from concourse.bass_utils import run_bass_kernel_spmd


F32 = mybir.dt.float32
BF16 = mybir.dt.bfloat16
AF = mybir.ActivationFunctionType
OP = mybir.AluOpType
AX = mybir.AxisListType

GAMMA = 0.99
NCORES = 8
BATCH = 4
SEQ = 4096
SC = SEQ // NCORES
T = SC * BATCH
NT = T // 128
HID = 1024
NHC = HID // 128
MEM = 256
NMC = MEM // 128
NCH = 4
NE = 8
DFF = 4096
NFC = DFF // 128
G128 = float(GAMMA ** 128)


class _S(object):
    __slots__ = ("h", "n")

    def __init__(self, h):
        self.h = h
        self.n = 0


class _Prog(object):
    ENGS = ("sync", "gpsimd", "tensor", "vector", "scalar")

    def __init__(self):
        self.steps = {k: [] for k in self.ENGS}

    def em(self, eng, build, waits=(), inc=None, amt=1):
        w = tuple((s.h, int(v)) for (s, v) in waits if v > 0)
        tgt = None
        if inc is not None:
            inc.n += amt
            tgt = (inc.h, amt)
        self.steps[eng].append((w, build, tgt))
        return None if inc is None else inc.n

    def replay(self, block):
        def mk(steps):
            def run(eng):
                for w, build, tgt in steps:
                    for h, v in w:
                        eng.wait_ge(h, v)
                    ins = build(eng)
                    if tgt is not None:
                        ins.then_inc(tgt[0], tgt[1])
            return run
        block.sync(mk(self.steps["sync"]))
        block.gpsimd(mk(self.steps["gpsimd"]))
        block.tensor(mk(self.steps["tensor"]))
        block.vector(mk(self.steps["vector"]))
        block.scalar(mk(self.steps["scalar"]))


def _np_consts():
    sj = (np.arange(512) // BATCH)[:, None].astype(np.float64)
    si = (np.arange(512) // BATCH)[None, :].astype(np.float64)
    DT = np.where(sj < si, GAMMA ** (si - sj - 1.0) / BATCH, 0.0).astype(np.float32)
    p = np.arange(128)
    gsc = np.zeros((128, 8), dtype=np.float32)
    for rt in range(4):
        s_loc = (32 * rt + p // BATCH).astype(np.float64)
        gsc[:, rt] = (GAMMA ** (127.0 - s_loc) / BATCH).astype(np.float32)
        gsc[:, 4 + rt] = (GAMMA ** s_loc).astype(np.float32)
    return DT, gsc


def build_nc(debug=False):
    nc = bass.Bass("TRN2", target_bir_lowering=False)

    x_d = nc.dram_tensor("x", [T, HID], F32, kind="ExternalInput").ap()
    wqT_d = nc.dram_tensor("wqT", [HID, MEM], F32, kind="ExternalInput").ap()
    wkT_d = nc.dram_tensor("wkT", [HID, MEM], F32, kind="ExternalInput").ap()
    wvT_d = nc.dram_tensor("wvT", [HID, MEM], F32, kind="ExternalInput").ap()
    woT_d = nc.dram_tensor("woT", [MEM, HID], F32, kind="ExternalInput").ap()
    m0T_d = nc.dram_tensor("m0T", [MEM, MEM], F32, kind="ExternalInput").ap()
    gate_d = nc.dram_tensor("gate128", [128, 1], F32, kind="ExternalInput").ap()
    wr_d = nc.dram_tensor("wr", [HID, NE], F32, kind="ExternalInput").ap()
    br_d = nc.dram_tensor("br", [1, NE], F32, kind="ExternalInput").ap()
    b1r_d = nc.dram_tensor("b1r", [128, NE * NFC], F32, kind="ExternalInput").ap()
    b2_d = nc.dram_tensor("b2row", [1, NE * HID], F32, kind="ExternalInput").ap()
    w1_d = nc.dram_tensor("w1", [NE, HID, DFF], F32, kind="ExternalInput").ap()
    w2_d = nc.dram_tensor("w2", [NE, DFF, HID], F32, kind="ExternalInput").ap()
    coefs_d = nc.dram_tensor("coefs", [128, 16], F32, kind="ExternalInput").ap()
    y_d = nc.dram_tensor("y", [T, HID], F32, kind="ExternalOutput").ap()
    ag_in_d = nc.dram_tensor("ag_in", [MEM, MEM], F32, kind="Internal").ap()
    ag_out_d = nc.dram_tensor("ag_out", [NCORES * MEM, MEM], F32, kind="Internal").ap()

    dbg = {}
    if debug:
        for nm, shape, dt in (
            ("dbg_xlnT", [128, NHC * T], BF16), ("dbg_vtok", [128, NT * MEM], BF16),
            ("dbg_kT", [128, NMC * T], BF16), ("dbg_dmt", [128, NCH * 2 * MEM], F32),
            ("dbg_ac", [128, 2 * MEM], F32), ("dbg_mst", [128, NCH * 2 * MEM], F32),
            ("dbg_qT0", [128, NMC * 512], BF16), ("dbg_pT0", [128, 4 * 512], BF16),
            ("dbg_r", [128, NCH * 4 * MEM], BF16), ("dbg_x2", [128, NT * HID], F32),
            ("dbg_mask", [128, NT * NE], F32), ("dbg_xlnT2", [128, NHC * T], BF16),
            ("dbg_h1", [128, NFC * 1024], BF16),
        ):
            dbg[nm] = nc.dram_tensor(nm, shape, dt, kind="ExternalOutput").ap()

    DT_np, gsc_np = _np_consts()
    DT_d = nc.inline_tensor(np.asarray(DT_np, dtype=ml_dtypes.bfloat16), name="DTmask").ap()
    gsc_d = nc.inline_tensor(gsc_np, name="gscv").ap()
    idf_d = nc.inline_tensor(np.eye(128, dtype=np.float32), name="identf").ap()
    idb_d = nc.inline_tensor(np.eye(128, dtype=ml_dtypes.bfloat16), name="identb").ap()
    ones_d = nc.inline_tensor(np.ones((1, 128), dtype=ml_dtypes.bfloat16), name="onesrow").ap()

    stk = contextlib.ExitStack()
    sb = lambda name, shape, dt: stk.enter_context(nc.sbuf_tensor(name, shape, dt))

    with stk:
        xres = sb("xres", [128, NT * HID], F32)
        xlnT = sb("xlnT", [128, NHC * T], BF16)
        wr_s = sb("wr_s", [128, NHC * NE], BF16)
        b1r_s = sb("b1r_s", [128, NE * NFC], F32)
        br_s = sb("br_s", [1, NE], BF16)
        ones_s = sb("ones_s", [1, 128], BF16)
        rmax = sb("rmax", [128, 1], F32)
        rsum = sb("rsum", [128, 1], F32)
        rginv = sb("rginv", [128, 1], F32)
        rd8 = sb("rd8", [128, NE], F32)
        mask_s = sb("mask_s", [128, NT * NE], F32)
        w1buf = sb("w1buf", [128, 3 * 4096], BF16)
        b2buf = sb("b2buf", [1, HID], BF16)

        sems = {}
        for nm in ("LD", "GLD", "AGI", "CC", "AGL", "XL", "B2", "W1A", "W1B",
                   "W1C", "W2A", "W2B", "W2C", "W2D", "PE", "DV", "AC", "ST"):
            sems[nm] = _S(stk.enter_context(nc.semaphore(nm)))
        LD, GLD = sems["LD"], sems["GLD"]
        AGI, CC, AGL = sems["AGI"], sems["CC"], sems["AGL"]
        XL, B2 = sems["XL"], sems["B2"]
        W1S = (sems["W1A"], sems["W1B"], sems["W1C"])
        W2S = (sems["W2A"], sems["W2B"], sems["W2C"], sems["W2D"])
        PE, DV, AC, ST = sems["PE"], sems["DV"], sems["AC"], sems["ST"]

        free = {}

        def fset(key, sem):
            free[key] = (sem, sem.n)

        def fw(key):
            return [free[key]] if key in free else []

        pref = {}
        sq_guard = [0]

        def _emit_b2_load(e):
            P.em("gpsimd", lambda eng, e=e: eng.dma_start(
                out=b2buf[0:1, :], in_=b2_d[0:1, e * HID:(e + 1) * HID]),
                waits=fw(("b2buf",)), inc=B2, amt=16)
            return B2.n

        def _emit_w1_load(e, fq):
            par = fq % 3
            for hf in range(4):
                P.em("gpsimd", lambda eng, e=e, fq=fq, par=par, hf=hf: eng.dma_start(
                    out=w1buf[:, par * 4096 + hf * 1024:
                              par * 4096 + (hf + 1) * 1024].rearrange(
                        "p (hc j) -> p hc j", j=512),
                    in_=w1_d[e][hf * 256:(hf + 1) * 256,
                                fq * 512:(fq + 1) * 512].rearrange(
                        "(hc p) j -> p hc j", p=128)),
                    waits=(fw(("w1buf", par)) if hf == 0 else []),
                    inc=W1S[par], amt=16)
            return (W1S[par], W1S[par].n)

        with contextlib.ExitStack() as stkA:
            ec = stkA.enter_context
            identf = ec(nc.sbuf_tensor("identf_s", [128, 128], F32))
            gsc_s = ec(nc.sbuf_tensor("gsc_s", [128, 8], F32))
            xlnsc = ec(nc.sbuf_tensor("xlnsc", [128, 2 * HID], F32))
            sqdump = ec(nc.sbuf_tensor("sqdump", [128, HID], F32))
            lnsum = ec(nc.sbuf_tensor("lnsum", [128, NT], F32))
            lnss = ec(nc.sbuf_tensor("lnss", [128, NT], F32))
            lnm = ec(nc.sbuf_tensor("lnm", [128, NT], F32))
            lnvar = ec(nc.sbuf_tensor("lnvar", [128, NT], F32))
            lnsq = ec(nc.sbuf_tensor("lnsq", [128, NT], F32))
            lnrs = ec(nc.sbuf_tensor("lnrs", [128, NT], F32))
            lnt1 = ec(nc.sbuf_tensor("lnt1", [128, NT], F32))
            lnmm2 = ec(nc.sbuf_tensor("lnmm2", [128, NT], F32))
            wqT_s = ec(nc.sbuf_tensor("wqT_s", [128, NHC * MEM], BF16))
            wkT_s = ec(nc.sbuf_tensor("wkT_s", [128, NHC * MEM], BF16))
            wvT_s = ec(nc.sbuf_tensor("wvT_s", [128, NHC * MEM], BF16))
            woT_s = ec(nc.sbuf_tensor("woT_s", [128, NMC * HID], BF16))
            m0T_s = ec(nc.sbuf_tensor("m0T_s", [128, NMC * MEM], F32))
            coefs_s = ec(nc.sbuf_tensor("coefs_s", [128, 16], F32))
            gate_s = ec(nc.sbuf_tensor("gate_s", [128, 1], F32))
            gvec_s = ec(nc.sbuf_tensor("gvec_s", [128, 1], F32))
            identb = ec(nc.sbuf_tensor("identb_s", [128, 128], BF16))
            DT_s = ec(nc.sbuf_tensor("DT_s", [128, 4 * 512], BF16))
            vtok = ec(nc.sbuf_tensor("vtok", [128, NT * MEM], BF16))
            kT_s = ec(nc.sbuf_tensor("kT_s", [128, NMC * T], BF16))
            ktil = ec(nc.sbuf_tensor("ktil", [128, 4 * MEM], BF16))
            dmt = ec(nc.sbuf_tensor("dmt", [128, NCH * 2 * MEM], BF16))
            ac_s = ec(nc.sbuf_tensor("ac_s", [128, 2 * MEM], F32))
            agbuf = ec(nc.sbuf_tensor("agbuf", [128, 512], F32))
            mst = ec(nc.sbuf_tensor("mst", [128, NCH * 2 * MEM], F32))
            mbf = ec(nc.sbuf_tensor("mbf", [128, 2 * MEM], BF16))
            qT_s = ec(nc.sbuf_tensor("qT_s", [128, NMC * 512], BF16))
            pTt = ec(nc.sbuf_tensor("pTt", [128, 4 * 512], BF16))
            rtok = ec(nc.sbuf_tensor("rtok", [128, 4 * MEM], BF16))
            rT_s = ec(nc.sbuf_tensor("rT_s", [128, NMC * 512], BF16))
            psTf0 = ec(nc.psum_tensor("psTf0", [128, 128], F32))
            psTf1 = ec(nc.psum_tensor("psTf1", [128, 128], F32))
            psTb0 = ec(nc.psum_tensor("psTb0", [128, 128], BF16))
            psTb1 = ec(nc.psum_tensor("psTb1", [128, 128], BF16))
            psC = ec(nc.psum_tensor("psC", [128, MEM], F32))
            psD = ec(nc.psum_tensor("psD", [128, MEM], F32))
            psA = ec(nc.psum_tensor("psA", [128, 512], F32))
            psB = ec(nc.psum_tensor("psB", [128, 512], F32))
            blockA = ec(nc.Block())
            P = _Prog()

            def dma(dst, src):
                return lambda eng, d=dst, s=src: eng.dma_start(out=d, in_=s)

            P.em("sync", dma(identf[:, :], idf_d), inc=LD, amt=16)
            P.em("sync", dma(identb[:, :], idb_d), inc=LD, amt=16)
            P.em("sync", dma(ones_s[:, :], ones_d), inc=LD, amt=16)
            P.em("sync", dma(DT_s[:].rearrange("p (r i) -> p r i", i=512),
                             DT_d.rearrange("(r p) i -> p r i", p=128)), inc=LD, amt=16)
            P.em("sync", dma(gsc_s[:, :], gsc_d), inc=LD, amt=16)
            P.em("sync", dma(b1r_s[:, :], b1r_d), inc=LD, amt=16)
            P.em("sync", dma(coefs_s[:, :], coefs_d), inc=LD, amt=16)
            P.em("sync", dma(m0T_s[:].rearrange("p (mc m) -> p mc m", m=MEM),
                             m0T_d.rearrange("(mc p) m -> p mc m", p=128)), inc=LD, amt=16)
            P.em("sync", dma(gate_s[:, :], gate_d), inc=LD, amt=16)
            gate_ld = LD.n
            P.em("sync", dma(xres[:].rearrange("p (t j) -> p t j", j=HID),
                             x_d.rearrange("(t p) j -> p t j", p=128)), inc=XL, amt=16)
            LD_ALL = LD.n

            for dst, src in ((wqT_s, wqT_d), (wkT_s, wkT_d), (wvT_s, wvT_d)):
                P.em("gpsimd", dma(dst[:].rearrange("p (hc m) -> p hc m", m=MEM),
                                   src.rearrange("(hc p) m -> p hc m", p=128)), inc=GLD, amt=16)
            P.em("gpsimd", dma(woT_s[:].rearrange("p (mc h) -> p mc h", h=HID),
                               woT_d.rearrange("(mc p) h -> p mc h", p=128)), inc=GLD, amt=16)
            P.em("gpsimd", dma(wr_s[:].rearrange("p (hc e) -> p hc e", e=NE),
                               wr_d.rearrange("(hc p) e -> p hc e", p=128)), inc=GLD, amt=16)
            P.em("gpsimd", dma(br_s[:, :], br_d), inc=GLD, amt=16)
            GLD_W = GLD.n

            P.em("scalar", lambda eng: eng.activation(gvec_s[:, :], gate_s[:, :], AF.Sigmoid),
                 waits=[(LD, gate_ld)], inc=AC)
            gv_ready = AC.n

            def emit_ln_stats_group(tiles, waits_red=(), waits_sq=()):
                for i, t in enumerate(tiles):
                    P.em("vector", lambda eng, t=t: eng.reduce_sum(
                        lnsum[:, t:t + 1], xres[:, t * HID:(t + 1) * HID], axis=AX.X),
                        waits=(list(waits_red) if i == 0 else []),
                        inc=(DV if i == len(tiles) - 1 else None))
                    P.em("scalar", lambda eng, t=t: eng.activation(
                        sqdump[:, :], xres[:, t * HID:(t + 1) * HID], AF.Square,
                        accum_out=lnss[:, t:t + 1]),
                        waits=(list(waits_sq) if i == 0 else [(AC, AC.n)]),
                        inc=AC)
                sq_guard[0] = AC.n
                return DV.n, AC.n

            def emit_ln_chain(t0, n, red_done, sq_done):
                c = slice(t0, t0 + n)
                P.em("vector", lambda eng: eng.tensor_scalar(
                    lnm[:, c], lnsum[:, c], 1.0 / HID, None, OP.mult),
                    waits=[(DV, red_done), (AC, sq_done)], inc=DV)
                P.em("vector", lambda eng: eng.tensor_mul(lnt1[:, c], lnm[:, c], lnm[:, c]),
                     waits=[(DV, DV.n)], inc=DV)
                P.em("vector", lambda eng: eng.tensor_scalar(
                    lnvar[:, c], lnss[:, c], 1.0 / HID, 1e-5, OP.mult, OP.add), inc=DV)
                P.em("vector", lambda eng: eng.tensor_sub(lnvar[:, c], lnvar[:, c], lnt1[:, c]),
                     waits=[(DV, DV.n)], inc=DV)
                var_ready = DV.n
                P.em("scalar", lambda eng: eng.activation(lnsq[:, c], lnvar[:, c], AF.Sqrt),
                     waits=[(DV, var_ready)], inc=AC)
                sq2 = AC.n
                P.em("vector", lambda eng: eng.reciprocal(lnrs[:, c], lnsq[:, c]),
                     waits=[(AC, sq2)], inc=DV)
                P.em("vector", lambda eng: eng.tensor_mul(lnt1[:, c], lnrs[:, c], lnrs[:, c]),
                     waits=[(DV, DV.n)], inc=DV)
                P.em("vector", lambda eng: eng.tensor_mul(lnt1[:, c], lnt1[:, c], lnvar[:, c]),
                     waits=[(DV, DV.n)], inc=DV)
                P.em("vector", lambda eng: eng.tensor_scalar(
                    lnt1[:, c], lnt1[:, c], -0.5, 1.5, OP.mult, OP.add),
                    waits=[(DV, DV.n)], inc=DV)
                P.em("vector", lambda eng: eng.tensor_mul(lnrs[:, c], lnrs[:, c], lnt1[:, c]),
                     waits=[(DV, DV.n)], inc=DV)
                P.em("vector", lambda eng: eng.tensor_mul(lnmm2[:, c], lnm[:, c], lnrs[:, c]),
                     waits=[(DV, DV.n)], inc=DV)
                P.em("vector", lambda eng: eng.tensor_scalar(
                    lnmm2[:, c], lnmm2[:, c], -1.0, None, OP.mult),
                    waits=[(DV, DV.n)], inc=DV)
                return DV.n

            def emit_ln_tile(t, ps_slices, ps_keys):
                par = t % 2
                w = fw(("xlnsc", par)) + [(DV, DV.n)]
                P.em("vector", lambda eng, t=t, par=par: eng.tensor_scalar(
                    xlnsc[:, par * HID:(par + 1) * HID], xres[:, t * HID:(t + 1) * HID],
                    lnrs[:, t:t + 1], lnmm2[:, t:t + 1], OP.mult, OP.add),
                    waits=w, inc=DV)
                dv_xln = DV.n
                for hc in range(NHC):
                    bank = hc % 2
                    w = ([(DV, dv_xln)] if hc == 0 else []) + fw(ps_keys[bank])
                    P.em("tensor", lambda eng, hc=hc, bank=bank, par=par: eng.transpose(
                        ps_slices[bank],
                        xlnsc[:, par * HID + hc * 128: par * HID + (hc + 1) * 128],
                        identf[:, :]),
                        waits=w, inc=PE)
                    tp = PE.n
                    if hc == NHC - 1:
                        fset(("xlnsc", par), PE)
                    P.em("vector", lambda eng, hc=hc, bank=bank, t=t: eng.tensor_copy(
                        xlnT[:, hc * T + t * 128: hc * T + (t + 1) * 128], ps_slices[bank]),
                        waits=[(PE, tp)], inc=DV)
                    fset(ps_keys[bank], DV)
                return DV.n

            rd, sd = emit_ln_stats_group(
                range(NT), [(XL, 16), (LD, LD_ALL)], [(XL, 16), (LD, LD_ALL)])
            emit_ln_chain(0, NT, rd, sd)
            tile_evac = [0] * NT
            tile2_evac = [0] * NT
            for t in range(NT):
                tile_evac[t] = emit_ln_tile(t, (psTf0[:, :], psTf1[:, :]),
                                            ("psTf0", "psTf1"))

                w = [(DV, tile_evac[t]), (GLD, GLD_W)] + fw("psC")
                for hc in range(NHC):
                    P.em("tensor", lambda eng, hc=hc, t=t: eng.matmul(
                        psC[:, :],
                        xlnT[:, hc * T + t * 128: hc * T + (t + 1) * 128],
                        wvT_s[:, hc * MEM:(hc + 1) * MEM],
                        start=(hc == 0), stop=(hc == NHC - 1)),
                        waits=(w if hc == 0 else []), inc=(PE if hc == NHC - 1 else None))
                vmm = PE.n
                P.em("vector", lambda eng, t=t: eng.tensor_copy(
                    vtok[:, t * MEM:(t + 1) * MEM], psC[:, :]), waits=[(PE, vmm)], inc=DV)
                fset("psC", DV)

                w = fw("psD")
                for hc in range(NHC):
                    P.em("tensor", lambda eng, hc=hc, t=t: eng.matmul(
                        psD[:, :],
                        xlnT[:, hc * T + t * 128: hc * T + (t + 1) * 128],
                        wkT_s[:, hc * MEM:(hc + 1) * MEM],
                        start=(hc == 0), stop=(hc == NHC - 1)),
                        waits=(w if hc == 0 else []), inc=(PE if hc == NHC - 1 else None))
                kmm = PE.n
                rt = t % 4
                P.em("vector", lambda eng, rt=rt: eng.tensor_scalar(
                    ktil[:, rt * MEM:(rt + 1) * MEM], psD[:, :], gsc_s[:, rt:rt + 1], None,
                    OP.mult), waits=[(PE, kmm)], inc=DV)
                fset("psD", DV)
                ktil_ready = DV.n

                if rt == 3:
                    h = t // 4
                    for mc in range(NMC):
                        ps = psC if mc == 0 else psD
                        pskey = "psC" if mc == 0 else "psD"
                        w = [(DV, ktil_ready)] + fw(pskey)
                        for rr in range(4):
                            P.em("tensor", lambda eng, rr=rr, mc=mc, h=h, ps=ps: eng.matmul(
                                ps[:, :],
                                ktil[:, rr * MEM + mc * 128: rr * MEM + (mc + 1) * 128],
                                vtok[:, (4 * h + rr) * MEM:(4 * h + rr + 1) * MEM],
                                start=(rr == 0), stop=(rr == 3)),
                                waits=(w if rr == 0 else []), inc=(PE if rr == 3 else None))
                        dmm = PE.n
                        P.em("vector", lambda eng, h=h, mc=mc, ps=ps: eng.tensor_copy(
                            dmt[:, (h * 2 + mc) * MEM:(h * 2 + mc + 1) * MEM], ps[:, :]),
                            waits=[(PE, dmm)], inc=DV)
                        fset(pskey, DV)

                    for mc in range(NMC):
                        ps = psA if mc == 0 else psB
                        pskey = "psA" if mc == 0 else "psB"
                        w = [(DV, tile_evac[t])] + fw(pskey)
                        for hc in range(NHC):
                            P.em("tensor", lambda eng, hc=hc, mc=mc, h=h, ps=ps: eng.matmul(
                                ps[:, :],
                                wkT_s[:, hc * MEM + mc * 128: hc * MEM + (mc + 1) * 128],
                                xlnT[:, hc * T + h * 512: hc * T + (h + 1) * 512],
                                start=(hc == 0), stop=(hc == NHC - 1)),
                                waits=(w if hc == 0 else []), inc=(PE if hc == NHC - 1 else None))
                        kTmm = PE.n
                        P.em("vector", lambda eng, mc=mc, h=h, ps=ps: eng.tensor_copy(
                            kT_s[:, mc * T + h * 512: mc * T + (h + 1) * 512], ps[:, :]),
                            waits=[(PE, kTmm)], inc=DV)
                        fset(pskey, DV)

            P.em("vector", lambda eng: eng.scalar_tensor_tensor(
                ac_s[:, :], dmt[:, 0:512], G128, dmt[:, 512:1024], OP.mult, OP.add),
                waits=[(DV, DV.n)], inc=DV)
            P.em("vector", lambda eng: eng.scalar_tensor_tensor(
                ac_s[:, :], ac_s[:, :], G128, dmt[:, 1024:1536], OP.mult, OP.add),
                waits=[(DV, DV.n)], inc=DV)
            P.em("vector", lambda eng: eng.scalar_tensor_tensor(
                ac_s[:, :], ac_s[:, :], G128, dmt[:, 1536:2048], OP.mult, OP.add),
                waits=[(DV, DV.n)], inc=DV)
            ac_ready = DV.n

            if debug:
                for nm, src in (("dbg_xlnT", xlnT), ("dbg_vtok", vtok), ("dbg_kT", kT_s),
                                ("dbg_dmt", dmt), ("dbg_ac", ac_s)):
                    P.em("scalar", dma(dbg[nm], src[:, :]), waits=[(DV, ac_ready)],
                         inc=ST, amt=16)

            P.em("gpsimd", dma(ag_in_d.rearrange("(mc p) m -> p mc m", p=128),
                               ac_s[:].rearrange("p (mc m) -> p mc m", m=MEM)),
                 waits=[(DV, ac_ready)], inc=AGI, amt=16)
            P.em("gpsimd", lambda eng: eng.collective_compute(
                "AllGather", OP.bypass, replica_groups=[list(range(NCORES))],
                ins=[ag_in_d.opt()], outs=[ag_out_d.opt()]),
                waits=[(AGI, AGI.n)], inc=CC, amt=1)
            cc_done = CC.n
            for r in range(NCORES):
                w = ([(CC, cc_done)] if r == 0 else []) + fw(("agbuf",))
                P.em("gpsimd", dma(
                    agbuf[:, :].rearrange("p (mc m) -> p mc m", m=MEM),
                    ag_out_d[r * MEM:(r + 1) * MEM, :].rearrange("(mc p) m -> p mc m", p=128)),
                    waits=w, inc=AGL, amt=16)
                rank_ld = AGL.n
                if r == 0:
                    P.em("vector", lambda eng: eng.tensor_scalar(
                        mst[:, 0:512], agbuf[:, :], coefs_s[:, 0:1], None, OP.mult),
                        waits=[(AGL, rank_ld), (LD, LD_ALL)], inc=DV)
                else:
                    P.em("vector", lambda eng, r=r: eng.scalar_tensor_tensor(
                        mst[:, 0:512], agbuf[:, :], coefs_s[:, r:r + 1],
                        mst[:, 0:512], OP.mult, OP.add),
                        waits=[(AGL, rank_ld), (DV, DV.n)], inc=DV)
                fset(("agbuf",), DV)
            P.em("vector", lambda eng: eng.scalar_tensor_tensor(
                mst[:, 0:512], m0T_s[:, :], coefs_s[:, 8:9], mst[:, 0:512], OP.mult, OP.add),
                waits=[(DV, DV.n)], inc=DV)
            for hh in range(1, NCH):
                P.em("vector", lambda eng, hh=hh: eng.scalar_tensor_tensor(
                    mst[:, hh * 512:(hh + 1) * 512], mst[:, (hh - 1) * 512:hh * 512], G128,
                    dmt[:, (hh - 1) * 512:hh * 512], OP.mult, OP.add),
                    waits=[(DV, DV.n)], inc=DV)
            mst_ready = DV.n

            if debug:
                P.em("scalar", dma(dbg["dbg_mst"], mst[:, :]), waits=[(DV, mst_ready)],
                     inc=ST, amt=16)

            pref["id"] = (0, 0)
            pref["b2"] = _emit_b2_load(0)
            pref["w1"] = {fq0: _emit_w1_load(0, fq0) for fq0 in range(3)}

            dbg_st_gate = 0
            for h in range(NCH):
                P.em("vector", lambda eng, h=h: eng.tensor_copy(
                    mbf[:, :], mst[:, h * 512:(h + 1) * 512]),
                    waits=[(DV, mst_ready)] + fw(("mbf",)), inc=DV)
                mbf_ready = DV.n
                if debug and h > 0:
                    P.em("vector", lambda eng: eng.engine_nop(), waits=[(ST, dbg_st_gate)])
                qT_ready = [0, 0]
                for mc in range(NMC):
                    ps = psA if mc == 0 else psB
                    pskey = "psA" if mc == 0 else "psB"
                    w = fw(pskey) + [(GLD, GLD_W)]
                    for hc in range(NHC):
                        P.em("tensor", lambda eng, hc=hc, mc=mc, h=h, ps=ps: eng.matmul(
                            ps[:, :],
                            wqT_s[:, hc * MEM + mc * 128: hc * MEM + (mc + 1) * 128],
                            xlnT[:, hc * T + h * 512: hc * T + (h + 1) * 512],
                            start=(hc == 0), stop=(hc == NHC - 1)),
                            waits=(w if hc == 0 else []), inc=(PE if hc == NHC - 1 else None))
                    qmm = PE.n
                    P.em("vector", lambda eng, mc=mc, ps=ps: eng.tensor_copy(
                        qT_s[:, mc * 512:(mc + 1) * 512], ps[:, :]), waits=[(PE, qmm)], inc=DV)
                    fset(pskey, DV)
                    qT_ready[mc] = DV.n

                pt_ready = 0
                for rkt in range(4):
                    ps = psA if rkt % 2 == 0 else psB
                    pskey = "psA" if rkt % 2 == 0 else "psB"
                    w = [(DV, max(qT_ready))] + fw(pskey)
                    for mc in range(NMC):
                        P.em("tensor", lambda eng, rkt=rkt, mc=mc, h=h, ps=ps: eng.matmul(
                            ps[:, :],
                            kT_s[:, mc * T + (4 * h + rkt) * 128: mc * T + (4 * h + rkt + 1) * 128],
                            qT_s[:, mc * 512:(mc + 1) * 512],
                            start=(mc == 0), stop=(mc == NMC - 1)),
                            waits=(w if mc == 0 else []), inc=(PE if mc == NMC - 1 else None))
                    smm = PE.n
                    P.em("vector", lambda eng, rkt=rkt, ps=ps: eng.tensor_tensor(
                        pTt[:, rkt * 512:(rkt + 1) * 512], ps[:, :],
                        DT_s[:, rkt * 512:(rkt + 1) * 512], OP.mult),
                        waits=[(PE, smm)], inc=DV)
                    fset(pskey, DV)
                    pt_ready = DV.n

                for rt in range(4):
                    w = [(DV, pt_ready)] + fw("psC")
                    for rkt in range(4):
                        P.em("tensor", lambda eng, rkt=rkt, rt=rt, h=h: eng.matmul(
                            psC[:, :],
                            pTt[:, rkt * 512 + rt * 128: rkt * 512 + (rt + 1) * 128],
                            vtok[:, (4 * h + rkt) * MEM:(4 * h + rkt + 1) * MEM],
                            start=(rkt == 0), stop=(rkt == 3)),
                            waits=(w if rkt == 0 else []), inc=(PE if rkt == 3 else None))
                    intra_mm = PE.n
                    w = [(DV, mbf_ready)] + fw("psD")
                    for mc in range(NMC):
                        P.em("tensor", lambda eng, mc=mc, rt=rt, h=h: eng.matmul(
                            psD[:, :],
                            qT_s[:, mc * 512 + rt * 128: mc * 512 + (rt + 1) * 128],
                            mbf[:, mc * MEM:(mc + 1) * MEM],
                            start=(mc == 0), stop=(mc == NMC - 1)),
                            waits=(w if mc == 0 else []), inc=(PE if mc == NMC - 1 else None))
                    inter_mm = PE.n
                    if rt == 3:
                        fset(("mbf",), PE)
                    P.em("vector", lambda eng: eng.tensor_copy(sqdump[:, 0:MEM], psC[:, :]),
                         waits=[(PE, intra_mm), (AC, sq_guard[0])], inc=DV)
                    fset("psC", DV)
                    P.em("vector", lambda eng, rt=rt: eng.scalar_tensor_tensor(
                        rtok[:, rt * MEM:(rt + 1) * MEM], psD[:, :], gsc_s[:, 4 + rt:5 + rt],
                        sqdump[:, 0:MEM], OP.mult, OP.add),
                        waits=[(PE, inter_mm), (DV, DV.n)], inc=DV)
                    fset("psD", DV)
                    r_ready = DV.n

                    for mc in range(NMC):
                        bank = mc % 2
                        psb = psTb0 if bank == 0 else psTb1
                        pskey = "psTb0" if bank == 0 else "psTb1"
                        w = [(DV, r_ready)] + fw(pskey)
                        P.em("tensor", lambda eng, mc=mc, rt=rt, psb=psb: eng.transpose(
                            psb[:, :], rtok[:, rt * MEM + mc * 128: rt * MEM + (mc + 1) * 128],
                            identb[:, :]), waits=w, inc=PE)
                        tmm = PE.n
                        P.em("vector", lambda eng, mc=mc, rt=rt, psb=psb: eng.tensor_copy(
                            rT_s[:, mc * 512 + rt * 128: mc * 512 + (rt + 1) * 128], psb[:, :]),
                            waits=[(PE, tmm)], inc=DV)
                        fset(pskey, DV)
                    rT_ready = DV.n

                    for hg in range(2):
                        ps = psA if hg == 0 else psB
                        pskey = "psA" if hg == 0 else "psB"
                        w = [(DV, rT_ready)] + fw(pskey)
                        for mc in range(NMC):
                            P.em("tensor", lambda eng, mc=mc, rt=rt, hg=hg, ps=ps: eng.matmul(
                                ps[:, :],
                                rT_s[:, mc * 512 + rt * 128: mc * 512 + (rt + 1) * 128],
                                woT_s[:, mc * HID + hg * 512: mc * HID + (hg + 1) * 512],
                                start=(mc == 0), stop=(mc == NMC - 1)),
                                waits=(w if mc == 0 else []), inc=(PE if mc == NMC - 1 else None))
                        omm = PE.n
                        tt = 4 * h + rt
                        P.em("vector", lambda eng, tt=tt, hg=hg, ps=ps: eng.scalar_tensor_tensor(
                            xres[:, tt * HID + hg * 512: tt * HID + (hg + 1) * 512],
                            ps[:, :], gvec_s[:, :],
                            xres[:, tt * HID + hg * 512: tt * HID + (hg + 1) * 512],
                            OP.mult, OP.add),
                            waits=[(PE, omm), (AC, gv_ready)], inc=DV)
                        fset(pskey, DV)

                if debug:
                    if h == 0:
                        P.em("scalar", dma(dbg["dbg_qT0"], qT_s[:, :]),
                             waits=[(DV, DV.n)], inc=ST, amt=16)
                        P.em("scalar", dma(dbg["dbg_pT0"], pTt[:, :]),
                             waits=[(DV, DV.n)], inc=ST, amt=16)
                    P.em("scalar", lambda eng, h=h: eng.dma_start(
                        out=dbg["dbg_r"][:, h * 1024:(h + 1) * 1024], in_=rtok[:, :]),
                        waits=[(DV, DV.n)], inc=ST, amt=16)
                    dbg_st_gate = ST.n

                resid_dv = DV.n
                tl = [4 * h + i for i in range(4)]
                rd2, sd2 = emit_ln_stats_group(tl, [(DV, resid_dv)], [(DV, resid_dv)])
                emit_ln_chain(4 * h, 4, rd2, sd2)
                for t in tl:
                    tile2_evac[t] = emit_ln_tile(t, (psTf0[:, :], psTf1[:, :]),
                                                 ("psTf0", "psTf1"))

            phaseA_pe_done = PE.n
            phaseA_dv_done = DV.n
            phaseA_ac_done = AC.n
            dbg_st_A = ST.n
            P.replay(blockA)

        TGT = 1024
        NTG = T // TGT
        with contextlib.ExitStack() as stkB:
            ec = stkB.enter_context
            h1sb = ec(nc.sbuf_tensor("h1sb", [128, NFC * TGT], BF16))
            w2buf = ec(nc.sbuf_tensor("w2buf", [128, 4 * 2048], BF16))
            psFF = ec(nc.psum_tensor("psFF", [128, 4096], F32))
            blockB = ec(nc.Block())
            P = _Prog()

            def dma(dst, src):
                return lambda eng, d=dst, s=src: eng.dma_start(out=d, in_=s)

            pb_guard = [(DV, phaseA_dv_done), (AC, phaseA_ac_done)]
            if debug:
                P.em("scalar", dma(dbg["dbg_x2"], xres[:, :]),
                     waits=[(DV, phaseA_dv_done)], inc=ST, amt=16)
                pb_guard = pb_guard + [(ST, ST.n)]

            def emit_router_all():
                base = 2560
                w0 = [(DV, phaseA_dv_done), (GLD, GLD_W), (DV, tile2_evac[NT - 1])]
                for t in range(NT):
                    rps = psFF[:, base + t * NE: base + (t + 1) * NE]
                    for hc in range(NHC):
                        P.em("tensor", lambda eng, hc=hc, t=t, rps=rps: eng.matmul(
                            rps, xlnT[:, hc * T + t * 128: hc * T + (t + 1) * 128],
                            wr_s[:, hc * NE:(hc + 1) * NE],
                            start=(hc == 0), stop=False),
                            waits=(w0 if (t == 0 and hc == 0) else []))
                    P.em("tensor", lambda eng, rps=rps: eng.matmul(
                        rps, ones_s[:, :], br_s[:, :], start=False, stop=True),
                        inc=(PE if t == NT - 1 else None))
                lg_done = PE.n
                for t in range(NT):
                    rps = psFF[:, base + t * NE: base + (t + 1) * NE]
                    P.em("vector", lambda eng, rps=rps: eng.reduce_max(
                        rmax[:, :], rps, axis=AX.X, negate=True),
                        waits=([(PE, lg_done)] if t == 0 else [(DV, DV.n), (AC, AC.n)]),
                        inc=DV)
                    negmax_ready = DV.n
                    P.em("scalar", lambda eng, rps=rps: eng.activation(
                        rd8[:, :], rps, AF.Exp, bias=rmax[:, :], accum_out=rsum[:, :]),
                        waits=[(DV, negmax_ready)] + ([(AC, AC.n)] if t > 0 else []),
                        inc=AC)
                    exp_ready = AC.n
                    P.em("vector", lambda eng: eng.reciprocal(rginv[:, :], rsum[:, :]),
                         waits=[(AC, exp_ready)], inc=DV)
                    P.em("vector", lambda eng, rps=rps: eng.tensor_scalar(
                        rd8[:, :], rps, rmax[:, :], None, OP.add),
                        waits=[(DV, DV.n)], inc=DV)
                    P.em("vector", lambda eng, t=t: eng.tensor_scalar(
                        mask_s[:, t * NE:(t + 1) * NE], rd8[:, :], 0.0, rginv[:, :],
                        OP.is_equal, OP.mult), waits=[(DV, DV.n)], inc=DV)

            first_w2load = [True]

            def emit_ffn_expert(tg, e, y_guard=None, nxt=None):
                xln_tg = tile2_evac[8 * tg + 7]
                w1_rec = [0] * 8
                if pref.get("id") == (tg, e):
                    b2_ld = pref["b2"]
                    for fq0 in range(3):
                        w1_rec[fq0] = pref["w1"][fq0]
                    pref.clear()
                else:
                    b2_ld = _emit_b2_load(e)
                    for fq0 in range(3):
                        w1_rec[fq0] = _emit_w1_load(e, fq0)

                def w1_load(fq):
                    w1_rec[fq] = _emit_w1_load(e, fq)

                h1_done = 0
                for fq in range(8):
                    par = fq % 3
                    for fc4 in range(4):
                        fc = fq * 4 + fc4
                        slot = fc % 2
                        skey = ("psffh", slot)
                        w = ([w1_rec[fq], (DV, xln_tg)] if fc4 == 0 else [])
                        w += fw(skey)
                        if fq == 0 and fc4 < 2:
                            w += fw(("psffYb", 2 * slot)) + fw(("psffYb", 2 * slot + 1))
                        for hc in range(NHC):
                            for th in range(2):
                                P.em("tensor", lambda eng, hc=hc, fc4=fc4, th=th, par=par,
                                     tg=tg, slot=slot: eng.matmul(
                                    psFF[:, slot * 1024 + th * 512: slot * 1024 + (th + 1) * 512],
                                    w1buf[:, par * 4096 + hc * 512 + fc4 * 128:
                                          par * 4096 + hc * 512 + (fc4 + 1) * 128],
                                    xlnT[:, hc * T + tg * TGT + th * 512:
                                         hc * T + tg * TGT + (th + 1) * 512],
                                    start=(hc == 0), stop=(hc == NHC - 1),
                                    skip_group_check=True),
                                    waits=(w if (hc == 0 and th == 0) else []),
                                    inc=(PE if (hc == NHC - 1 and th == 1) else None))
                        hmm_ = PE.n
                        if fc4 == 3:
                            free[("w1buf", par)] = (PE, PE.n)
                        P.em("scalar", lambda eng, fc=fc, e=e, slot=slot: eng.activation(
                            h1sb[:, fc * TGT:(fc + 1) * TGT],
                            psFF[:, slot * 1024:(slot + 1) * 1024],
                            AF.Relu, bias=b1r_s[:, e * NFC + fc: e * NFC + fc + 1]),
                            waits=[(PE, hmm_)], inc=AC)
                        free[("psffh", slot)] = (AC, AC.n)
                        h1_done = AC.n
                    if fq + 3 < 8:
                        w1_load(fq + 3)

                if debug and tg == 0 and e == 0:
                    P.em("scalar", dma(dbg["dbg_h1"], h1sb[:, :]),
                         waits=[(AC, h1_done)], inc=ST, amt=16)
                    P.em("scalar", lambda eng: eng.nop(), waits=[(ST, ST.n)])

                w2_pairs = [(g, fq) for g in range(2) for fq in range(8)]
                w2_rec = {}

                def w2_load(idx):
                    g, fq = w2_pairs[idx]
                    par = idx % 4
                    key = ("w2buf", par)
                    if first_w2load[0]:
                        free[key] = (DV, phaseA_dv_done)
                        first_w2load[0] = False
                    for hf in range(4):
                        P.em("gpsimd", dma(
                            w2buf[:, par * 2048 + hf * 512:
                                  par * 2048 + (hf + 1) * 512].rearrange(
                                "p (f j) -> p f j", j=512),
                            w2_d[e][fq * 512 + hf * 128:fq * 512 + (hf + 1) * 128,
                                    g * 512:(g + 1) * 512].rearrange(
                                "(f p) j -> p f j", p=128)),
                            waits=(fw(key) if hf == 0 else []),
                            inc=W2S[par], amt=16)
                    w2_rec[(g, fq)] = (W2S[par], W2S[par].n)

                for up in range(4):
                    w2_load(up)
                for g in range(2):
                    w = [(B2, b2_ld)]
                    if y_guard is not None and g == 0:
                        w += [(DV, y_guard[0]), (AC, y_guard[1])]
                    for t8 in range(8):
                        P.em("tensor", lambda eng, t8=t8, g=g: eng.matmul(
                            psFF[:, t8 * 512:(t8 + 1) * 512], ones_s[:, :],
                            b2buf[0:1, g * 512:(g + 1) * 512],
                            start=True, stop=False, skip_group_check=True),
                            waits=((w if t8 == 0 else []) + [(AC, h1_done)]
                                   + fw(("psffYb", t8))))
                    for fq in range(8):
                        idx = g * 8 + fq
                        par = idx % 4
                        w = [w2_rec[(g, fq)]]
                        for fc4 in range(4):
                            fc = fq * 4 + fc4
                            for t8 in range(8):
                                last = (fc == NFC - 1)
                                P.em("tensor", lambda eng, fc=fc, fc4=fc4, t8=t8, par=par,
                                     g=g, last=last: eng.matmul(
                                    psFF[:, t8 * 512:(t8 + 1) * 512],
                                    h1sb[:, fc * TGT + t8 * 128: fc * TGT + (t8 + 1) * 128],
                                    w2buf[:, par * 2048 + fc4 * 512:
                                          par * 2048 + (fc4 + 1) * 512],
                                    start=False, stop=last, skip_group_check=True),
                                    waits=(w if (fc4 == 0 and t8 == 0) else []),
                                    inc=(PE if (fc4 == 3 and t8 == 7) else None))
                        free[("w2buf", par)] = (PE, PE.n)
                        if idx + 4 < 16:
                            w2_load(idx + 4)
                        if g == 1 and fq == 0 and nxt is not None:
                            free[("b2buf",)] = (PE, PE.n)
                            pref["id"] = nxt
                            pref["b2"] = _emit_b2_load(nxt[1])
                            pref["w1"] = {fq0: _emit_w1_load(nxt[1], fq0)
                                          for fq0 in range(3)}
                    if g == 1:
                        free[("b2buf",)] = (PE, PE.n)
                    y_done = PE.n

                    for t8 in range(8):
                        tt = 8 * tg + t8
                        P.em("vector", lambda eng, tt=tt, t8=t8, g=g, e=e:
                             eng.scalar_tensor_tensor(
                                 xres[:, tt * HID + g * 512: tt * HID + (g + 1) * 512],
                                 psFF[:, t8 * 512:(t8 + 1) * 512],
                                 mask_s[:, tt * NE + e: tt * NE + e + 1],
                                 xres[:, tt * HID + g * 512: tt * HID + (g + 1) * 512],
                                 OP.mult, OP.add),
                             waits=[(PE, y_done)], inc=DV)
                        free[("psffYb", t8)] = (DV, DV.n)

            def emit_store(tg):
                st_w = [(DV, DV.n)]
                P.em("scalar", lambda eng, tg=tg: eng.dma_start(
                    out=y_d[tg * TGT:(tg + 1) * TGT, :].rearrange("(t p) j -> p t j", p=128),
                    in_=xres[:, tg * 8 * HID:(tg + 1) * 8 * HID].rearrange(
                        "p (t j) -> p t j", j=HID)),
                    waits=st_w, inc=ST, amt=16)

            emit_router_all()
            router_dv_done, router_ac_done = DV.n, AC.n
            if debug:
                P.em("scalar", dma(dbg["dbg_mask"], mask_s[:, :]),
                     waits=[(DV, router_dv_done)], inc=ST, amt=16)
                P.em("scalar", dma(dbg["dbg_xlnT2"], xlnT[:, :]),
                     waits=[(DV, router_dv_done)], inc=ST, amt=16)
            order = [(tg, e) for tg in range(NTG) for e in range(NE)]
            for i, (tg, e) in enumerate(order):
                emit_ffn_expert(tg, e,
                                y_guard=(router_dv_done, router_ac_done)
                                if (tg == 0 and e == 0) else None,
                                nxt=(order[i + 1] if i + 1 < len(order) else None))
                if e == NE - 1:
                    emit_store(tg)

            P.em("scalar", lambda eng: eng.nop(), waits=[(ST, ST.n)])
            P.replay(blockB)

    return nc


_NC_CACHE = {}


def kernel(**inputs):
    x = np.asarray(inputs["L0_attn_x"], dtype=np.float32)
    Wq = np.asarray(inputs["L0_attn_Wq"], dtype=np.float32)
    Wk = np.asarray(inputs["L0_attn_Wk"], dtype=np.float32)
    Wv = np.asarray(inputs["L0_attn_Wv"], dtype=np.float32)
    Wo = np.asarray(inputs["L0_attn_Wo"], dtype=np.float32)
    gate = np.asarray(inputs["L0_attn_gate"], dtype=np.float32)
    M0 = np.asarray(inputs["L0_attn_M0"], dtype=np.float32)
    Wr = np.asarray(inputs["L0_ffn_Wr"], dtype=np.float32)
    br = np.asarray(inputs["L0_ffn_br"], dtype=np.float32)
    W1 = np.asarray(inputs["L0_ffn_W1"], dtype=np.float32)
    b1 = np.asarray(inputs["L0_ffn_b1"], dtype=np.float32)
    W2 = np.asarray(inputs["L0_ffn_W2"], dtype=np.float32)
    b2 = np.asarray(inputs["L0_ffn_b2"], dtype=np.float32)

    if "nc" not in _NC_CACHE:
        _NC_CACHE["nc"] = build_nc()
    nc = _NC_CACHE["nc"]

    wqT = np.ascontiguousarray(Wq.T)
    wkT = np.ascontiguousarray(Wk.T)
    wvT = np.ascontiguousarray(Wv.T)
    woT = np.ascontiguousarray(Wo.T)
    m0T = np.ascontiguousarray(M0.T)
    gate128 = np.ascontiguousarray(np.tile(gate.reshape(1, 1), (128, 1)))
    b1r = np.ascontiguousarray(b1.reshape(NE, NFC, 128).transpose(2, 0, 1).reshape(128, NE * NFC))
    b2row = np.ascontiguousarray(b2.reshape(1, NE * HID))
    brr = np.ascontiguousarray(br.reshape(1, NE))
    w1c = np.ascontiguousarray(W1)
    w2c = np.ascontiguousarray(W2)

    in_maps = []
    for c in range(NCORES):
        xc = np.ascontiguousarray(
            x[:, c * SC:(c + 1) * SC, :].transpose(1, 0, 2).reshape(T, HID))
        coefs = np.zeros((16,), dtype=np.float64)
        for cp in range(c):
            coefs[cp] = GAMMA ** (SC * (c - 1 - cp))
        coefs[8] = GAMMA ** (SC * c)
        coefs128 = np.ascontiguousarray(
            np.tile(coefs.astype(np.float32).reshape(1, 16), (128, 1)))
        in_maps.append({
            "x": xc, "wqT": wqT, "wkT": wkT, "wvT": wvT, "woT": woT,
            "m0T": m0T, "gate128": gate128, "wr": Wr, "br": brr,
            "b1r": b1r, "b2row": b2row, "w1": w1c, "w2": w2c,
            "coefs": coefs128,
        })

    res = run_bass_kernel_spmd(nc, in_maps, core_ids=list(range(NCORES)))
    out = np.empty((BATCH, SEQ, HID), dtype=np.float32)
    for c in range(NCORES):
        shard = np.asarray(res.results[c]["y"])
        out[:, c * SC:(c + 1) * SC, :] = shard.reshape(SC, BATCH, HID).transpose(1, 0, 2)
    return out
